# version 14 (final)
# speedup vs baseline: 1.2112x; 1.0246x over previous
from contextlib import ExitStack

import numpy as np
import ml_dtypes
import concourse.bass as bass
import concourse.mybir as mybir
from concourse.bass_utils import run_bass_kernel_spmd

F32 = mybir.dt.float32
BF16 = mybir.dt.bfloat16
ALU = mybir.AluOpType
ACT = mybir.ActivationFunctionType
AX = mybir.AxisListType

D = 2048
NH, HD = 16, 128
BLK = 256
TOK = 1024
NBLK = TOK // BLK
FC = D // 128
TC = TOK // 128
HFF = 8192
LNEPS = 1e-5
NCORES = 8
HC = 512
NHC = HFF // HC
NHS = HC // 128


class Sem:
    def __init__(self, es, nc, name):
        self.h = es.enter_context(nc.semaphore(name))
        self.n = 0

    def inc(self, instr, k=1):
        instr.then_inc(self.h, k)
        self.n += k
        return instr

    def dma(self, instr):
        return self.inc(instr, 16)

    def wait(self, eng):
        if self.n > 0:
            eng.wait_ge(self.h, self.n)

    def wait_at(self, eng, mark):
        if mark > 0:
            eng.wait_ge(self.h, mark)


class Bank:
    def __init__(self, es, nc, idx):
        self.t = es.enter_context(nc.psum_tensor(f"pb{idx}", [128, 512], F32))
        self.done = Sem(es, nc, f"pbd{idx}")
        self.free = Sem(es, nc, f"pbf{idx}")

    def acquire(self, eng):
        self.free.wait(eng)

    def produced(self, instr):
        self.done.inc(instr)
        return instr

    def consumed_wait(self, eng):
        self.done.wait(eng)

    def release(self, instr):
        self.free.inc(instr)
        return instr


def build_program(taps=()):
    taps = set(taps)
    import os
    nc = bass.Bass("TRN2", target_bir_lowering=False, debug=False,
                   detect_race_conditions=bool(os.environ.get("KRACE")))
    es = ExitStack()

    x_in = nc.dram_tensor("x", [TOK, D], F32, kind="ExternalInput").ap()
    slope_in = nc.dram_tensor("slope", [1, NH], F32, kind="ExternalInput").ap()
    normw_in = nc.dram_tensor("normw", [D], F32, kind="ExternalInput").ap()
    wq0_in = nc.dram_tensor("wq0", [3 * HD, D], F32, kind="ExternalInput").ap()
    wq1_in = nc.dram_tensor("wq1", [3 * HD, D], F32, kind="ExternalInput").ap()
    wg_in = nc.dram_tensor("wg", [D // 8, D], F32, kind="ExternalInput").ap()
    wo_in = nc.dram_tensor("wo", [D // 8, D], F32, kind="ExternalInput").ap()
    wgu_in = nc.dram_tensor("wgu", [2 * HFF // 8, D], F32, kind="ExternalInput").ap()
    wpj_in = nc.dram_tensor("wpj", [D // 8, HFF], F32, kind="ExternalInput").ap()
    gaps_in = nc.dram_tensor("gaps", [1, NCORES], F32, kind="ExternalInput").ap()
    incm_in = nc.dram_tensor("incm", [1, NCORES], F32, kind="ExternalInput").ap()
    out_ext = nc.dram_tensor("out", [TOK, D], F32, kind="ExternalOutput").ap()

    tap_aps = {}

    def tap(name, shape, dtype=F32):
        if name in taps:
            tap_aps[name] = nc.dram_tensor("tap_" + name, shape, dtype,
                                           kind="ExternalOutput").ap()
            return tap_aps[name]
        return None

    agq0i = nc.dram_tensor("agq0i", [2, D, 3 * HD], BF16)
    agq0o = nc.dram_tensor("agq0o", [NCORES, 2, D, 3 * HD], BF16)
    agq1i = nc.dram_tensor("agq1i", [2, D, 3 * HD], BF16)
    agq1o = nc.dram_tensor("agq1o", [NCORES, 2, D, 3 * HD], BF16)
    agwgi = nc.dram_tensor("agwgi", [2, D, D // 8], BF16)
    agwgo = nc.dram_tensor("agwgo", [NCORES, 2, D, D // 8], BF16)
    agwoi = nc.dram_tensor("agwoi", [2, D, D // 8], BF16)
    agwoo = nc.dram_tensor("agwoo", [4, NCORES, D // 2, D // 8], BF16)
    aggui = nc.dram_tensor("aggui", [D, 2 * HFF // 8], BF16)
    agguo = nc.dram_tensor("agguo", [4, NCORES, D // 4, 2 * HFF // 8], BF16)
    agpji = nc.dram_tensor("agpji", [HFF, D // 8], BF16)
    agpjo = nc.dram_tensor("agpjo", [2, NCORES, HFF // 2, D // 8], BF16)
    agli = nc.dram_tensor("agli", [NH, HD, HD], F32)
    aglo = nc.dram_tensor("aglo", [NH, NCORES, HD, HD], F32)
    mskspill = nc.dram_tensor("mskspill", [NH, 3, 128, BLK], F32)
    x2sp = nc.dram_tensor("x2sp", [TOK, D], F32)

    sm_names = [
        "s_small", "s_const", "s_coef", "s_mske", "s_mskm", "s_mskw",
        "s_xload", "s_lnv", "s_lns", "s_app", "s_spl", "s_xn",
        "s_wshl", "s_wsh1", "s_wtsp", "s_ws0", "s_ws1", "s_ws2", "s_ws3",
        "s_ql", "s_ql1", "s_qt", "s_qq", "s_qv", "s_qgu", "s_qpj",
        "s_ag", "s_wload", "s_wl1", "s_wgl", "s_mskp0", "s_mskp1", "s_silu",
        "s_sgt", "s_qkv", "s_pjl",
        "s_tp", "s_qsr", "s_oev", "s_upd", "s_lw",
        "s_ing", "s_inb", "s_corr", "s_gate", "s_z", "s_msku",
        "s_wol", "s_apv", "s_aps", "s_x2", "s_x2w", "s_xn2",
        "s_wfl", "s_hh", "s_hsp", "s_yad", "s_fin", "s_out", "s_eps",
    ]
    S = {n: Sem(es, nc, n) for n in sm_names}
    banks = [Bank(es, nc, i) for i in range(8)]
    ag_k = [0]

    idn = es.enter_context(nc.sbuf_tensor("idn", [128, 128], F32))
    ones2 = es.enter_context(nc.sbuf_tensor("ones2", [128, 2], BF16))
    ones2T = es.enter_context(nc.sbuf_tensor("ones2T", [2, 128], BF16))
    onesr = es.enter_context(nc.sbuf_tensor("onesr", [1, 128], F32))
    sbc = es.enter_context(nc.sbuf_tensor("sbc", [128, NH], F32))
    sneg = es.enter_context(nc.sbuf_tensor("sneg", [128, NH], F32))
    nwc = es.enter_context(nc.sbuf_tensor("nwc", [128, NH], F32))
    gb = es.enter_context(nc.sbuf_tensor("gb", [128, 2 * NCORES], F32))
    coefs = es.enter_context(nc.sbuf_tensor("coefs", [128, NH, NCORES], F32))
    ctmp = es.enter_context(nc.sbuf_tensor("ctmp", [128, NH], F32))
    bdp = es.enter_context(nc.sbuf_tensor("bdp", [128, NH, NBLK], F32))
    bdv = es.enter_context(nc.sbuf_tensor("bdv", [128, NH], F32))
    kdall = es.enter_context(nc.sbuf_tensor("kdall", [128, NH, 2], F32))
    stat = es.enter_context(nc.sbuf_tensor("stat", [128, 48], F32))
    srow = es.enter_context(nc.sbuf_tensor("srow", [1, 128], F32))
    onec = es.enter_context(nc.sbuf_tensor("onec", [128, 1], F32))
    msqa = es.enter_context(nc.sbuf_tensor("msqa", [1, TOK], F32))
    msqb = es.enter_context(nc.sbuf_tensor("msqb", [1, TOK], F32))
    epsc = es.enter_context(nc.sbuf_tensor("epsc", [128, 2], F32))
    xnh = es.enter_context(nc.sbuf_tensor("xnh", [128, FC, TOK], BF16))
    xnl = es.enter_context(nc.sbuf_tensor("xnl", [128, FC, TOK], BF16))
    pre_es = ExitStack()
    iq = pre_es.enter_context(nc.sbuf_tensor("iq", [128, BLK], F32))
    idxT = pre_es.enter_context(nc.sbuf_tensor("idxT", [128, 2, BLK], F32))
    czT = pre_es.enter_context(nc.sbuf_tensor("czT", [128, 2, BLK], F32))
    idxP = pre_es.enter_context(nc.sbuf_tensor("idxP", [128, 2, BLK], F32))
    ikd = pre_es.enter_context(nc.sbuf_tensor("ikd", [128, 2], F32))

    with nc.Block() as _blk:
        sync, scal, vect, tens, gps = nc.sync, nc.scalar, nc.vector, nc.tensor, nc.gpsimd

        def collective(ins, outs):
            S["s_ag"].inc(gps.collective_compute(
                "AllGather", ALU.bypass, replica_groups=[list(range(NCORES))],
                ins=[ins], outs=[outs]))
            ag_k[0] += 1
            return ag_k[0]

        gps.iota(idxT[:, 0, :128], pattern=[[1, 128]], base=0, channel_multiplier=0,
                 allow_small_or_imprecise_dtypes=True)
        S["s_small"].inc(gps.iota(idxT[:, 1, :128], pattern=[[0, 128]], base=0,
                                  channel_multiplier=1,
                                  allow_small_or_imprecise_dtypes=True))
        S["s_small"].wait(vect)
        vect.tensor_tensor(idn[:], idxT[:, 0, :128], idxT[:, 1, :128], ALU.is_equal)
        S["s_const"].inc(vect.drain())
        gps.wait_ge(S["s_const"].h, 1)
        gps.iota(iq[:], pattern=[[1, BLK]], base=1, channel_multiplier=0,
                 allow_small_or_imprecise_dtypes=True)
        for j in range(2):
            gps.iota(idxT[:, j, :], pattern=[[1, BLK]], base=-128 * j,
                     channel_multiplier=-1, allow_small_or_imprecise_dtypes=True)
        S["s_small"].inc(gps.iota(ikd[:], pattern=[[-128, 2]], base=BLK - 1,
                                  channel_multiplier=-1,
                                  allow_small_or_imprecise_dtypes=True))
        S["s_const"].wait(vect)
        S["s_small"].wait(vect)
        vect.memset(onesr[:], 1.0)
        vect.memset(srow[:], 0.0)
        vect.memset(onec[:], 1.0)
        vect.memset(msqa[:], 0.0)
        vect.memset(msqb[:], 0.0)
        gps.iota(czT[:, 0, 0:2], pattern=[[-64, 2]], base=0, channel_multiplier=1,
                 allow_small_or_imprecise_dtypes=True)
        S["s_small"].inc(gps.iota(czT[:2, 0, 2:130], pattern=[[1, 128]], base=0,
                                  channel_multiplier=-64,
                                  allow_small_or_imprecise_dtypes=True))
        S["s_small"].wait(vect)
        vect.tensor_single_scalar(stat[:, 44:46], czT[:, 0, 0:2], 0.0, ALU.is_ge)
        vect.tensor_single_scalar(stat[:, 46:48], czT[:, 0, 0:2], 64.0, ALU.is_lt)
        vect.drain()
        vect.tensor_tensor(ones2[:], stat[:, 44:46], stat[:, 46:48], ALU.mult)
        vect.tensor_single_scalar(czT[:2, 1, 0:128], czT[:2, 0, 2:130], 0.0, ALU.is_ge)
        vect.tensor_single_scalar(czT[:2, 1, 128:256], czT[:2, 0, 2:130], 64.0,
                                  ALU.is_lt)
        vect.drain()
        vect.tensor_tensor(ones2T[:], czT[:2, 1, 0:128], czT[:2, 1, 128:256], ALU.mult)
        for j in range(2):
            vect.tensor_single_scalar(czT[:, j, :], idxT[:, j, :], 0.0, ALU.is_ge)
            vect.tensor_single_scalar(idxP[:, j, :], idxT[:, j, :], 0.0, ALU.max)
        S["s_const"].inc(vect.drain())

        S["s_const"].wait_at(sync, 2)
        S["s_xload"].dma(sync.dma_start(out=srow[:, :NH], in_=slope_in[:, :]))
        S["s_xload"].dma(sync.dma_start(out=srow[:, 32:32 + NCORES], in_=gaps_in[:, :]))
        S["s_xload"].dma(sync.dma_start(out=srow[:, 48:48 + NCORES], in_=incm_in[:, :]))
        S["s_xload"].dma(sync.dma_start(
            out=nwc[:], in_=normw_in.rearrange("(h p) -> p h", p=128),
            allow_slow_non_contiguous=True))
        S["s_xload"].wait(tens)
        bk0 = banks[0]
        bk0.acquire(tens)
        bk0.produced(tens.matmul(bk0.t[:, :64], onesr[:], srow[:, :64],
                                 start=True, stop=True))
        bk0.consumed_wait(vect)
        vect.tensor_copy(sbc[:], bk0.t[:, :NH])
        vect.tensor_single_scalar(sneg[:], bk0.t[:, :NH], -1.0, ALU.mult)
        vect.tensor_copy(gb[:, :NCORES], bk0.t[:, 32:32 + NCORES])
        bk0.release(vect.tensor_copy(gb[:, NCORES:], bk0.t[:, 48:48 + NCORES]))
        S["s_const"].inc(vect.drain())

        S["s_const"].wait(scal)
        for t in range(NBLK):
            scal.activation(bdp[:, :, t], sbc[:], ACT.Exp, scale=-256.0 * t)
        scal.activation(bdv[:], sbc[:], ACT.Exp, scale=-256.0)
        for h in range(NH):
            for j in range(2):
                scal.activation(kdall[:, h, j:j + 1], ikd[:, j:j + 1], ACT.Exp,
                                scale=sneg[:, h:h + 1])
        S["s_mske"].inc(scal.drain())
        for c in range(NCORES):
            S["s_mske"].wait(vect)
            vect.drain()
            vect.tensor_scalar(ctmp[:], sbc[:], gb[:, c:c + 1], None, ALU.mult)
            S["s_coef"].inc(vect.drain())
            S["s_coef"].wait(scal)
            scal.activation(ctmp[:], ctmp[:], ACT.Exp, scale=-1024.0)
            S["s_mske"].inc(scal.drain())
            S["s_mske"].wait(vect)
            vect.tensor_scalar(coefs[:, :, c], ctmp[:],
                               gb[:, NCORES + c:NCORES + c + 1], None, ALU.mult)
            S["s_coef"].inc(vect.drain())

        mbld = pre_es.enter_context(nc.sbuf_tensor("mbld", [128, 3, BLK], F32))
        for h in range(NH):
            S["s_mskw"].wait_at(scal, 16 * 3 * h)
            for j in range(2):
                scal.activation(mbld[:, j, :], idxP[:, j, :], ACT.Exp,
                                scale=sneg[:, h:h + 1])
            scal.activation(mbld[:, 2, :], iq[:], ACT.Exp, scale=sneg[:, h:h + 1])
            S["s_mske"].inc(scal.drain())
            S["s_mske"].wait(vect)
            for j in range(2):
                vect.tensor_tensor(mbld[:, j, :], mbld[:, j, :], czT[:, j, :], ALU.mult)
            vect.tensor_single_scalar(mbld[:, 2, :], mbld[:, 2, :], 1.0, ALU.mult)
            S["s_mskm"].inc(vect.drain())
            S["s_mskm"].wait(sync)
            for j in range(3):
                S["s_mskw"].dma(sync.dma_start(out=mskspill.ap()[h, j],
                                               in_=mbld[:, j, :]))

        pre_es.close()

        def emit_ln(x_tile, ncols, sq_tile, eps_col=None):
            vect.tensor_tensor(sq_tile, x_tile, x_tile, ALU.mult)
            vect.tensor_reduce(stat[:, 0:1].rearrange("p (a b) -> p a b", b=1),
                               x_tile.rearrange("p (a b) -> p a b", a=1),
                               AX.X, ALU.add)
            vect.tensor_reduce(stat[:, 1:2].rearrange("p (a b) -> p a b", b=1),
                               sq_tile.rearrange("p (a b) -> p a b", a=1),
                               AX.X, ALU.add)
            vect.tensor_single_scalar(stat[:, 3:4], stat[:, 0:1], 1.0 / ncols,
                                      ALU.mult)
            vect.tensor_single_scalar(stat[:, 4:5], stat[:, 1:2], 1.0 / ncols,
                                      ALU.mult)
            vect.drain()
            vect.scalar_tensor_tensor(stat[:, 4:5], stat[:, 3:4], stat[:, 3:4],
                                      stat[:, 4:5], ALU.mult, ALU.subtract)
            vect.drain()
            if eps_col is None:
                vect.tensor_scalar(stat[:, 4:5], stat[:, 4:5], -1.0, LNEPS,
                                   ALU.mult, ALU.add)
            else:
                vect.tensor_single_scalar(stat[:, 4:5], stat[:, 4:5], -1.0, ALU.mult)
                vect.drain()
                vect.tensor_tensor(stat[:, 4:5], stat[:, 4:5], eps_col, ALU.add)
            vect.drain()
            vect.reciprocal(stat[:, 5:6], stat[:, 4:5])
            S["s_lnv"].inc(vect.drain())
            S["s_lnv"].wait(scal)
            scal.sqrt(stat[:, 6:7], stat[:, 5:6])
            S["s_lns"].inc(scal.drain())
            S["s_lns"].wait(vect)
            vect.tensor_scalar(stat[:, 7:8], stat[:, 3:4], stat[:, 6:7], -1.0,
                               ALU.mult, ALU.mult)
            S["s_lnv"].inc(vect.drain())
            S["s_lnv"].wait(scal)

        def transpose_split(src_tile, dst_hi, dst_lo, bank_sel):
            bk = banks[bank_sel]
            bk.acquire(tens)
            tens.transpose(bk.t[:, :128], src_tile, idn[:])
            bk.produced(tens.drain())
            bk.consumed_wait(vect)
            vect.tensor_copy(dst_hi, bk.t[:, :128])
            vect.drain()
            vect.tensor_tensor(dst_lo, bk.t[:, :128], dst_hi, ALU.subtract)
            bk.release(vect.drain())

        with ExitStack() as p0:
            xtmp = p0.enter_context(nc.sbuf_tensor("xtmp", [128, 2, D], F32))
            xnt = p0.enter_context(nc.sbuf_tensor("xnt", [128, D], F32))
            app_mark = [0, 0]
            for tc in range(TC):
                buf = tc % 2
                S["s_app"].wait_at(sync, app_mark[buf])
                S["s_xload"].dma(sync.dma_start(
                    out=xtmp[:, buf, :], in_=x_in[128 * tc:128 * (tc + 1), :]))
                S["s_xload"].wait(vect)
                S["s_xn"].wait(vect)
                emit_ln(xtmp[:, buf, :], D, xnt[:])
                vect.tensor_scalar(xnt[:], xtmp[:, buf, :], stat[:, 6:7],
                                   stat[:, 7:8], ALU.mult, ALU.add)
                S["s_app"].inc(vect.drain())
                app_mark[buf] = S["s_app"].n
                if tc == 0:
                    t_ = tap("xt0", [128, D], F32)
                    if t_ is not None:
                        S["s_xload"].wait(sync)
                        S["s_out"].dma(sync.dma_start(out=t_, in_=xtmp[:, 0, :]))
                    t_ = tap("stat0", [128, 48], F32)
                    if t_ is not None:
                        S["s_app"].wait(sync)
                        S["s_out"].dma(sync.dma_start(out=t_, in_=stat[:]))
                    t_ = tap("xnt0", [128, D], F32)
                    if t_ is not None:
                        S["s_app"].wait(sync)
                        S["s_out"].dma(sync.dma_start(out=t_, in_=xnt[:]))
                S["s_const"].wait(tens)
                S["s_app"].wait(tens)
                for g in range(FC // 4):
                    bk = banks[g % 2]
                    bk.acquire(tens)
                    for q in range(4):
                        fc = 4 * g + q
                        tens.transpose(bk.t[:, 128 * q:128 * (q + 1)],
                                       xnt[:, 128 * fc:128 * (fc + 1)], idn[:])
                    bk.produced(tens.drain())
                    bk.consumed_wait(vect)
                    pv = bk.t[:].rearrange("p (g c) -> p g c", c=128)
                    vect.tensor_copy(xnh[:, 4 * g:4 * g + 4,
                                         128 * tc:128 * (tc + 1)], pv)
                    vect.drain()
                    vect.tensor_tensor(xnl[:, 4 * g:4 * g + 4,
                                          128 * tc:128 * (tc + 1)], pv,
                                       xnh[:, 4 * g:4 * g + 4,
                                           128 * tc:128 * (tc + 1)],
                                       ALU.subtract)
                    bk.release(vect.drain())
                S["s_xn"].inc(vect.drain())

        t_ = tap("xn1h", [128, FC, TOK], BF16)
        if t_ is not None:
            S["s_xn"].wait(sync)
            S["s_out"].dma(sync.dma_start(out=t_, in_=xnh[:]))
            S["s_out"].dma(sync.dma_start(out=tap("xn1l", [128, FC, TOK], BF16),
                                          in_=xnl[:]))

        with ExitStack() as pw:
            wsh = pw.enter_context(nc.sbuf_tensor("wsh", [128, 2, 3, D], F32))
            wtt = pw.enter_context(nc.sbuf_tensor("wtt", [128, FC, 2, 3 * HD], BF16))
            WS = [S["s_ws0"], S["s_ws1"], S["s_ws2"], S["s_ws3"]]
            WSH = [S["s_wshl"], S["s_wsh1"]]
            shards = [(wq0_in, 3 * HD, agq0i), (wq1_in, 3 * HD, agq1i),
                      (wg_in, D // 8, agwgi), (wo_in, D // 8, agwoi)]
            wtsp_mark = [0, 0]

            def issue_wsh(si):
                w_ap, rows, _ = shards[si]
                sl = si % 2
                S["s_xn"].wait(sync)
                S["s_wtsp"].wait_at(sync, wtsp_mark[sl])
                for rc in range(rows // 128):
                    WSH[sl].dma(sync.dma_start(
                        out=wsh[:, sl, rc, :],
                        in_=w_ap[128 * rc:128 * (rc + 1), :]))

            def prep_shard(si):
                _, rows, agi = shards[si]
                sl = si % 2
                nrc = rows // 128
                WSH[sl].wait(tens)
                if si > 0:
                    WS[si - 1].wait(vect)
                for fc in range(FC):
                    bk = banks[2 + fc % 2]
                    bk.acquire(tens)
                    for rc in range(nrc):
                        tens.transpose(bk.t[:, 128 * rc:128 * (rc + 1)],
                                       wsh[:, sl, rc, 128 * fc:128 * (fc + 1)],
                                       idn[:])
                    bk.produced(tens.drain())
                    bk.consumed_wait(vect)
                    vect.tensor_copy(wtt[:, fc, 0, :rows], bk.t[:, :rows])
                    vect.drain()
                    vect.tensor_tensor(wtt[:, fc, 1, :rows], bk.t[:, :rows],
                                       wtt[:, fc, 0, :rows], ALU.subtract)
                    bk.release(vect.drain())
                    S["s_qv"].inc(vect.drain())
                    S["s_qv"].wait(sync)
                    for pl in range(2):
                        WS[si].dma(sync.dma_start(
                            out=agi.ap()[pl, 128 * fc:128 * (fc + 1), :rows],
                            in_=wtt[:, fc, pl, :rows]))
                S["s_wtsp"].inc(tens.drain())
                wtsp_mark[sl] = S["s_wtsp"].n

            issue_wsh(0)
            issue_wsh(1)
            prep_shard(0)
            issue_wsh(2)
            WS[0].wait(gps)
            k_agq0 = collective(agq0i.ap().opt(), agq0o.ap().opt())
            prep_shard(1)
            issue_wsh(3)
            WS[1].wait(gps)
            k_agq1 = collective(agq1i.ap().opt(), agq1o.ap().opt())
            prep_shard(2)
            WS[2].wait(gps)
            k_agwg = collective(agwgi.ap().opt(), agwgo.ap().opt())
            prep_shard(3)

        with ExitStack() as pq:
            qw = pq.enter_context(nc.sbuf_tensor("qw", [128, 2, 2, D], F32))
            qsc = pq.enter_context(nc.sbuf_tensor("qsc", [128, 2, 512], F32))
            qbt = pq.enter_context(nc.sbuf_tensor("qbt", [128, FC, 256], BF16))
            qhat = pq.enter_context(nc.sbuf_tensor("qhat", [128, FC, 256], BF16))
            qs = pq.enter_context(nc.sbuf_tensor("qs", [2, 4, 512], F32))
            qsb = pq.enter_context(nc.sbuf_tensor("qsb", [2, 3, 512], BF16))
            qd2 = pq.enter_context(nc.sbuf_tensor("qd2", [128, 512], BF16))
            QL = [S["s_ql"], S["s_ql1"]]
            qw_use = [0, 0]

            def quant_shard(w_ap, nrows, ncolt, agi, ssem):
                agiv = agi.ap().rearrange("(cb kc p) r -> cb p kc r", kc=FC, p=128)
                blocks = [(rb, cb) for rb in range(nrows // 256)
                          for cb in range(ncolt // D)]

                def issue_qw(bi):
                    rb, cb = blocks[bi]
                    sl_ = bi % 2
                    S["s_qt"].wait_at(sync, qw_use[sl_])
                    for half in range(2):
                        QL[sl_].dma(sync.dma_start(
                            out=qw[:, sl_, half, :],
                            in_=w_ap[256 * rb + 128 * half:
                                     256 * rb + 128 * (half + 1),
                                     D * cb:D * (cb + 1)]))

                issue_qw(0)
                if len(blocks) > 1:
                    issue_qw(1)
                for bi, (rb, cb) in enumerate(blocks):
                    sl = bi % 2
                    QL[sl].wait(tens)
                    S["s_qq"].wait(tens)
                    for g in range(FC // 2):
                        b1 = banks[4 + g % 2]
                        b1.acquire(tens)
                        for tq in range(4):
                            kc, half = 2 * g + tq // 2, tq % 2
                            tens.transpose(
                                b1.t[:, 128 * tq:128 * (tq + 1)],
                                qw[:, sl, half, 128 * kc:128 * (kc + 1)], idn[:])
                        b1.produced(tens.drain())
                        b1.consumed_wait(vect)
                        vect.tensor_copy(qbt[:, 2 * g:2 * g + 2, :], b1.t[:])
                        b1.release(vect.drain())
                    S["s_qt"].inc(tens.drain())
                    qw_use[sl] = S["s_qt"].n
                    if bi + 2 < len(blocks):
                        issue_qw(bi + 2)
                    S["s_qgu"].wait(vect)
                    S["s_qpj"].wait(vect)
                    for kp in range(FC // 2):
                        til = qbt[:, 2 * kp:2 * kp + 2, :]
                        qh2 = qhat[:, 2 * kp:2 * kp + 2, :]
                        vect.tensor_single_scalar(qh2, til, -1.0, ALU.mult)
                        vect.drain()
                        vect.tensor_tensor(qh2, til, qh2, ALU.max)
                        S["s_silu"].inc(vect.drain())
                        S["s_silu"].wait(tens)
                        b2 = banks[6]
                        b2.acquire(tens)
                        b2.produced(tens.matmul(b2.t[:2, :512], ones2[:], qh2,
                                                start=True, stop=True))
                        b2.consumed_wait(vect)
                        vect.tensor_single_scalar(qsb[:, 0, :], b2.t[:2, :512],
                                                  1.0 / 64.0, ALU.mult)
                        b2.release(vect.drain())
                        vect.tensor_single_scalar(qsb[:, 0, :], qsb[:, 0, :],
                                                  1e-8, ALU.max)
                        vect.drain()
                        vect.tensor_copy(qs[:, 0, :], qsb[:, 0, :])
                        vect.drain()
                        vect.reciprocal(qs[:, 1, :], qs[:, 0, :])
                        vect.drain()
                        vect.tensor_copy(qsb[:, 1, :], qs[:, 1, :])
                        vect.drain()
                        vect.tensor_tensor(qs[:, 2, :], qs[:, 1, :],
                                           qsb[:, 1, :], ALU.subtract)
                        vect.drain()
                        vect.tensor_copy(qsb[:, 2, :], qs[:, 2, :])
                        S["s_qv"].inc(vect.drain())
                        S["s_qv"].wait(tens)
                        b3 = banks[7]
                        b3.acquire(tens)
                        tens.matmul(b3.t[:, :512], ones2T[:], qsb[:, 1, :],
                                    start=True, stop=False)
                        b3.produced(tens.matmul(b3.t[:, :512], ones2T[:],
                                                qsb[:, 2, :],
                                                start=False, stop=True))
                        b3.consumed_wait(vect)
                        vect.tensor_tensor(qh2, til, b3.t[:, :512], ALU.mult)
                        b3.release(vect.drain())
                        vect.tensor_single_scalar(qsc[:, 0, :], qh2, 0.5,
                                                  ALU.is_gt)
                        vect.tensor_single_scalar(qsc[:, 1, :], qh2, -0.5,
                                                  ALU.is_lt)
                        vect.drain()
                        vect.tensor_tensor(qh2, qsc[:, 0, :], qsc[:, 1, :],
                                           ALU.subtract)
                        S["s_qv"].inc(vect.drain())
                        S["s_qv"].wait(tens)
                        b4 = banks[6]
                        b4.acquire(tens)
                        b4.produced(tens.matmul(b4.t[:, :512], ones2T[:],
                                                qsb[:, 0, :],
                                                start=True, stop=True))
                        b4.consumed_wait(vect)
                        vect.tensor_tensor(qh2, qh2, b4.t[:, :512], ALU.mult)
                        b4.release(vect.drain())
                        vect.drain()
                        vect.tensor_tensor(qd2[:], qh2, til, ALU.subtract)
                        vect.drain()
                        vect.tensor_tensor(qh2, til, qd2[:], ALU.add)
                        S["s_qq"].inc(vect.drain())
                    S["s_qq"].wait(sync)
                    ssem.dma(sync.dma_start(
                        out=agiv[cb, :, :, 256 * rb:256 * (rb + 1)],
                        in_=qhat[:]))

            S["s_ws3"].wait(sync)
            S["s_wtsp"].wait(sync)
            quant_shard(wgu_in, 2 * HFF // 8, D, aggui, S["s_qgu"])
            quant_shard(wpj_in, D // 8, HFF, agpji, S["s_qpj"])

        k_agwo = k_aggu = k_agpj = None
        with ExitStack() as pa:
            zh = pa.enter_context(nc.sbuf_tensor("zh", [128, NH, TOK], BF16))
            zl = pa.enter_context(nc.sbuf_tensor("zl", [128, NH, TOK], BF16))
            pa2 = ExitStack()
            wtb = pa2.enter_context(nc.sbuf_tensor("wtb", [128, 2, 2, FC, 128],
                                                   BF16))
            qf = pa2.enter_context(nc.sbuf_tensor("qf", [128, 2, TOK], F32))
            kf = pa2.enter_context(nc.sbuf_tensor("kf", [128, TOK], F32))
            vf = pa2.enter_context(nc.sbuf_tensor("vf", [128, TOK], F32))
            gf = pa2.enter_context(nc.sbuf_tensor("gf", [128, TOK], F32))
            sgt = pa2.enter_context(nc.sbuf_tensor("sgt", [128, 512], F32))
            vtm = pa2.enter_context(nc.sbuf_tensor("vtm", [128, 2, 128], F32))
            kst = pa2.enter_context(nc.sbuf_tensor("kst", [128, 2, 128], F32))
            qkd = pa2.enter_context(nc.sbuf_tensor("qkd", [128, 2, 256], F32))
            qsb_ = pa2.enter_context(nc.sbuf_tensor("qsbuf", [128, 256], F32))
            st8 = pa2.enter_context(nc.sbuf_tensor("st8", [128, 128], F32))
            ing = pa2.enter_context(nc.sbuf_tensor("ing", [128, NCORES, 128], F32))
            inb = pa2.enter_context(nc.sbuf_tensor("inb", [128, 128], F32))
            ot = pa2.enter_context(nc.sbuf_tensor("ot", [128, 2, TOK], F32))
            sqb = pa2.enter_context(nc.sbuf_tensor("sqb", [128, TOK], F32))
            msk = pa2.enter_context(nc.sbuf_tensor("msk", [128, 2, 3, BLK], F32))

            msk_use = [0, 0]
            q_use = [0, 0]
            o_use = [0, 0]
            wtb_use = [0, 0]
            wslot = [0]
            WL = [S["s_wload"], S["s_wl1"]]

            def issue_wload(src_ap):
                sl = wslot[0] % 2
                wslot[0] += 1
                S["s_qkv"].wait_at(sync, wtb_use[sl])
                for pl in range(2):
                    WL[sl].dma(sync.dma_start(out=wtb[:, sl, pl, :, :],
                                              in_=src_ap(pl)))
                return sl

            def proj3(dst_slice, wsrc, th, silu):
                b = banks[th]
                b.acquire(tens)
                last = None
                for fc in range(FC):
                    tens.matmul(b.t[:], wsrc[:, 0, fc],
                                xnh[:, fc, 512 * th:512 * (th + 1)],
                                start=(fc == 0), stop=False)
                    tens.matmul(b.t[:], wsrc[:, 0, fc],
                                xnl[:, fc, 512 * th:512 * (th + 1)],
                                start=False, stop=False)
                    last = tens.matmul(b.t[:], wsrc[:, 1, fc],
                                       xnh[:, fc, 512 * th:512 * (th + 1)],
                                       start=False, stop=(fc == FC - 1))
                b.produced(last)
                b.consumed_wait(scal)
                S["s_sgt"].wait(scal)
                scal.activation(sgt[:], b.t[:], ACT.Sigmoid)
                S["s_silu"].inc(scal.drain())
                b.consumed_wait(vect)
                S["s_silu"].wait(vect)
                if silu:
                    vect.tensor_tensor(dst_slice, sgt[:], b.t[:], ALU.mult)
                else:
                    vect.tensor_copy(dst_slice, sgt[:])
                S["s_sgt"].inc(vect.drain())
                b.release(vect.drain())

            k_agl = [None] * NH

            def emit_C(h):
                hb = h % 2
                sync.wait_ge(S["s_ag"].h, k_agl[h])
                S["s_inb"].wait(sync)
                S["s_ing"].dma(sync.dma_start(
                    out=ing[:], in_=aglo.ap()[h].rearrange("c p e -> p c e")))
                S["s_ing"].wait(vect)
                S["s_corr"].wait(vect)
                vect.tensor_scalar(inb[:], ing[:, 0, :], coefs[:, h, 0:1], None,
                                   ALU.mult)
                for c in range(1, NCORES):
                    vect.drain()
                    vect.scalar_tensor_tensor(inb[:], ing[:, c, :],
                                              coefs[:, h, c:c + 1], inb[:],
                                              ALU.mult, ALU.add)
                S["s_inb"].inc(vect.drain())
                for b_ in range(NBLK):
                    c0 = BLK * b_
                    S["s_oev"].wait(vect)
                    S["s_corr"].wait(vect)
                    vect.tensor_tensor(qsb_[:], qf[:, hb, c0:c0 + BLK],
                                       msk[:, hb, 2, :], ALU.mult)
                    if b_ > 0:
                        vect.drain()
                        vect.tensor_scalar(qsb_[:], qsb_[:], bdp[:, h, b_:b_ + 1],
                                           None, ALU.mult)
                    S["s_qsr"].inc(vect.drain())
                    bc = banks[3]
                    bc.acquire(tens)
                    S["s_qsr"].wait(tens)
                    S["s_inb"].wait(tens)
                    bc.produced(tens.matmul(bc.t[:, :256], inb[:], qsb_[:],
                                            start=True, stop=True))
                    bc.consumed_wait(vect)
                    vect.tensor_tensor(ot[:, hb, c0:c0 + BLK],
                                       ot[:, hb, c0:c0 + BLK],
                                       bc.t[:, :256], ALU.add)
                    bc.release(vect.drain())
                    S["s_corr"].inc(vect.drain())
                q_use[hb] = S["s_corr"].n
                S["s_msku"].inc(vect.drain())
                S["s_corr"].wait(vect)
                vect.tensor_tensor(sqb[:], ot[:, hb, :], ot[:, hb, :], ALU.mult)
                S["s_qsr"].inc(vect.drain())
                for th in range(2):
                    bsq = banks[5]
                    bsq.acquire(tens)
                    S["s_qsr"].wait(tens)
                    bsq.produced(tens.matmul(bsq.t[:1, :512], onec[:],
                                             sqb[:, 512 * th:512 * (th + 1)],
                                             start=True, stop=True))
                    bsq.consumed_wait(vect)
                    vect.tensor_tensor(msqa[:, 512 * th:512 * (th + 1)],
                                       msqa[:, 512 * th:512 * (th + 1)],
                                       bsq.t[:1, :512], ALU.add)
                    bsq.release(vect.drain())
                S["s_oev"].inc(vect.drain())
                msk_use[hb] = S["s_msku"].n
                if h == 0:
                    t_ = tap("ot0", [128, TOK], F32)
                    if t_ is not None:
                        S["s_corr"].wait(sync)
                        S["s_out"].dma(sync.dma_start(out=t_, in_=ot[:, 0, :]))
                sync.wait_ge(S["s_ag"].h, k_agwg)
                sl = issue_wload(
                    lambda pl: agwgo.ap()[h // 2, pl, :,
                                          128 * (h % 2):128 * (h % 2 + 1)]
                    .rearrange("(fc p) o -> p fc o", p=128))
                WL[sl].wait(tens)
                for th in range(2):
                    proj3(gf[:, 512 * th:512 * (th + 1)], wtb[:, sl], th,
                          silu=False)
                S["s_qkv"].inc(tens.drain())
                wtb_use[sl] = S["s_qkv"].n
                if h == 0:
                    t_ = tap("gt0", [128, TOK], F32)
                    if t_ is not None:
                        S["s_sgt"].wait(sync)
                        S["s_out"].dma(sync.dma_start(out=t_, in_=gf[:]))
                        S["s_out"].wait(vect)
                    t_ = tap("wgb0", [128, 2, FC, 128], BF16)
                    if t_ is not None:
                        WL[sl].wait(sync)
                        S["s_out"].dma(sync.dma_start(out=t_, in_=wtb[:, sl]))
                S["s_corr"].wait(vect)
                vect.scalar_tensor_tensor(gf[:], gf[:], nwc[:, h:h + 1],
                                          ot[:, hb, :], ALU.mult, ALU.mult)
                vect.drain()
                vect.tensor_copy(zh[:, h, :], gf[:])
                vect.drain()
                vect.tensor_tensor(zl[:, h, :], gf[:], zh[:, h, :], ALU.subtract)
                S["s_z"].inc(vect.drain())
                o_use[hb] = S["s_z"].n

            for h in range(NH):
                hb = h % 2
                S["s_msku"].wait_at(sync, msk_use[hb])
                for j in range(3):
                    S[f"s_mskp{hb}"].dma(sync.dma_start(out=msk[:, hb, j, :],
                                                        in_=mskspill.ap()[h, j]))
                agw = agq0o if h < 8 else agq1o
                agk = k_agq0 if h < 8 else k_agq1
                S["s_corr"].wait_at(vect, q_use[hb])

                def issue_wtb(oc):
                    sync.wait_ge(S["s_ag"].h, agk)
                    return issue_wload(
                        lambda pl: agw.ap()[h % 8, pl, :, 128 * oc:128 * (oc + 1)]
                        .rearrange("(fc p) o -> p fc o", p=128))

                slots = [issue_wtb(0), issue_wtb(1), None]
                for oc in range(3):
                    sl = slots[oc]
                    WL[sl].wait(tens)
                    S["s_xn"].wait(tens)
                    for th in range(2):
                        dst = (qf[:, hb, 512 * th:512 * (th + 1)] if oc == 0 else
                               (kf[:, 512 * th:512 * (th + 1)] if oc == 1 else
                                vf[:, 512 * th:512 * (th + 1)]))
                        proj3(dst, wtb[:, sl], th, silu=True)
                    S["s_qkv"].inc(tens.drain())
                    wtb_use[sl] = S["s_qkv"].n
                    if oc == 0:
                        slots[2] = issue_wtb(2)

                if h == 0:
                    for nm, buf in (("qf0", None), ("kf0", None), ("vf0", None)):
                        t_ = tap(nm, [128, TOK], F32)
                        if t_ is not None:
                            S["s_qkv"].wait(sync)
                            src_ = qf[:, 0, :] if nm == "qf0" else (
                                kf[:] if nm == "kf0" else vf[:])
                            S["s_out"].dma(sync.dma_start(out=t_, in_=src_))
                S["s_lw"].wait(vect)
                vect.memset(st8[:], 0.0)
                S["s_upd"].inc(vect.drain())
                S[f"s_mskp{hb}"].wait(vect)
                for b_ in range(NBLK):
                    c0 = BLK * b_
                    for j in range(2):
                        bb = banks[2 + j]
                        bb.acquire(tens)
                        S["s_qkv"].wait(tens)
                        bb.produced(tens.transpose(
                            bb.t[:, :128], vf[:, c0 + 128 * j:c0 + 128 * (j + 1)],
                            idn[:]))
                        bb.consumed_wait(vect)
                        vect.tensor_copy(vtm[:, j, :], bb.t[:, :128])
                        bb.release(vect.drain())
                        S["s_tp"].inc(vect.drain())
                        bb2 = banks[4 + j]
                        bb2.acquire(tens)
                        bb2.produced(tens.transpose(
                            bb2.t[:, :128], kf[:, c0 + 128 * j:c0 + 128 * (j + 1)],
                            idn[:]))
                        bb2.consumed_wait(vect)
                        vect.tensor_scalar(kst[:, j, :], bb2.t[:, :128],
                                           kdall[:, h, j:j + 1], None, ALU.mult)
                        bb2.release(vect.drain())
                        S["s_tp"].inc(vect.drain())
                    S["s_oev"].wait(vect)
                    vect.tensor_tensor(qsb_[:], qf[:, hb, c0:c0 + BLK],
                                       msk[:, hb, 2, :], ALU.mult)
                    S["s_qsr"].inc(vect.drain())
                    for j in range(2):
                        bq = banks[6 if j == 0 else 0]
                        bq.acquire(tens)
                        bq.produced(tens.matmul(
                            bq.t[:, :256], kf[:, c0 + 128 * j:c0 + 128 * (j + 1)],
                            qf[:, hb, c0:c0 + BLK], start=True, stop=True))
                        bq.consumed_wait(vect)
                        vect.tensor_tensor(qkd[:, j, :], bq.t[:, :256],
                                           msk[:, hb, j, :], ALU.mult)
                        bq.release(vect.drain())
                    S["s_qsr"].inc(vect.drain())
                    bo = banks[7]
                    bo.acquire(tens)
                    S["s_tp"].wait(tens)
                    S["s_qsr"].wait(tens)
                    S["s_upd"].wait(tens)
                    tens.matmul(bo.t[:, :256], vtm[:, 0, :], qkd[:, 0, :],
                                start=True, stop=False)
                    tens.matmul(bo.t[:, :256], vtm[:, 1, :], qkd[:, 1, :],
                                start=False, stop=False)
                    bo.produced(tens.matmul(bo.t[:, :256], st8[:], qsb_[:],
                                            start=False, stop=True))
                    bs = banks[2]
                    bs.acquire(tens)
                    tens.matmul(bs.t[:, :128], kst[:, 0, :], vtm[:, 0, :],
                                start=True, stop=False)
                    bs.produced(tens.matmul(bs.t[:, :128], kst[:, 1, :], vtm[:, 1, :],
                                            start=False, stop=True))
                    bs.consumed_wait(vect)
                    vect.scalar_tensor_tensor(st8[:], st8[:], bdv[:, h:h + 1],
                                              bs.t[:, :128], ALU.mult, ALU.add)
                    bs.release(vect.drain())
                    S["s_upd"].inc(vect.drain())
                    bo.consumed_wait(vect)
                    S["s_z"].wait_at(vect, o_use[hb])
                    vect.tensor_copy(ot[:, hb, c0:c0 + BLK], bo.t[:, :256])
                    bo.release(vect.drain())
                    S["s_oev"].inc(vect.drain())
                S["s_upd"].wait(sync)
                S["s_lw"].dma(sync.dma_start(out=agli.ap()[h], in_=st8[:]))
                gps.wait_ge(S["s_lw"].h, S["s_lw"].n)
                k_agl[h] = collective(agli.ap()[h].opt(), aglo.ap()[h].opt())
                if 3 <= h <= 6:
                    ci = h - 3
                    pl_, rh_ = divmod(ci, 2)
                    if ci == 0:
                        S["s_ws3"].wait(gps)
                    k_agwo = collective(
                        agwoi.ap()[pl_, (D // 2) * rh_:(D // 2) * (rh_ + 1)].opt(),
                        agwoo.ap()[ci].opt())
                if 7 <= h <= 10:
                    gi = h - 7
                    if gi == 0:
                        S["s_qgu"].wait(gps)
                    k_aggu = collective(
                        aggui.ap()[(D // 4) * gi:(D // 4) * (gi + 1)].opt(),
                        agguo.ap()[gi].opt())
                if h == 11:
                    S["s_qpj"].wait(gps)
                    collective(agpji.ap()[:HFF // 2].opt(), agpjo.ap()[0].opt())
                if h == 12:
                    k_agpj = collective(agpji.ap()[HFF // 2:].opt(),
                                        agpjo.ap()[1].opt())

                if h > 0:
                    emit_C(h - 1)
            emit_C(NH - 1)

            t_ = tap("zh", [128, NH, TOK], BF16)
            if t_ is not None:
                S["s_z"].wait(sync)
                S["s_out"].dma(sync.dma_start(out=t_, in_=zh[:]))
                S["s_out"].dma(sync.dma_start(out=tap("zl", [128, NH, TOK], BF16),
                                              in_=zl[:]))

            pa2.close()
            with ExitStack() as po:
                wob = po.enter_context(nc.sbuf_tensor("wob", [128, FC, 512], BF16))
                apg = po.enter_context(nc.sbuf_tensor("apg", [128, 4, D], F32))
                xl2 = po.enter_context(nc.sbuf_tensor("xl2", [128, D], F32))
                x2t = po.enter_context(nc.sbuf_tensor("x2t", [128, D], F32))


                S["s_z"].wait(tens)
                for grp in range(2):
                    for ft in range(4):
                        S["s_z"].wait(sync)
                        S["s_apv"].wait(sync)
                        sync.wait_ge(S["s_ag"].h, k_agwo)
                        for sh2 in range(2):
                            for rh_ in range(2):
                                S["s_wol"].dma(sync.dma_start(
                                    out=wob[:, 8 * rh_:8 * (rh_ + 1),
                                            256 * sh2:256 * (sh2 + 1)],
                                    in_=agwoo.ap()[rh_, 2 * ft + sh2]
                                        .rearrange("(oc p) f -> p oc f", p=128)))
                        S["s_wol"].wait(tens)
                        for tci in range(4):
                            tc = 4 * grp + tci
                            bt = banks[tci]
                            bt.acquire(tens)
                            for oc in range(FC):
                                tens.matmul(bt.t[:],
                                            zh[:, oc, 128 * tc:128 * (tc + 1)],
                                            wob[:, oc, :],
                                            start=(oc == 0), stop=False)
                                tens.matmul(bt.t[:],
                                            zl[:, oc, 128 * tc:128 * (tc + 1)],
                                            wob[:, oc, :], start=False, stop=False)
                        S["s_apv"].inc(tens.drain())
                        S["s_apv"].wait(sync)
                        for sh2 in range(2):
                            for rh_ in range(2):
                                S["s_wol"].dma(sync.dma_start(
                                    out=wob[:, 8 * rh_:8 * (rh_ + 1),
                                            256 * sh2:256 * (sh2 + 1)],
                                    in_=agwoo.ap()[2 + rh_, 2 * ft + sh2]
                                        .rearrange("(oc p) f -> p oc f", p=128)))
                        S["s_wol"].wait(tens)
                        for tci in range(4):
                            tc = 4 * grp + tci
                            bt = banks[tci]
                            last = None
                            for oc in range(FC):
                                last = tens.matmul(
                                    bt.t[:], zh[:, oc, 128 * tc:128 * (tc + 1)],
                                    wob[:, oc, :], start=False,
                                    stop=(oc == FC - 1))
                            bt.produced(last)
                            bt.consumed_wait(vect)
                            vect.tensor_copy(apg[:, tci, 512 * ft:512 * (ft + 1)],
                                             bt.t[:])
                            bt.release(vect.drain())
                            S["s_aps"].inc(vect.drain())
                        S["s_apv"].inc(tens.drain())
                    for tci in range(4):
                        tc = 4 * grp + tci
                        S["s_oev"].wait(sync)
                        S["s_lnv"].wait(sync)
                        S["s_eps"].dma(sync.dma_start(
                            out=epsc[:, 0:1],
                            in_=msqa[0:1, 128 * tc:128 * (tc + 1)],
                            allow_slow_non_contiguous=True))
                        S["s_eps"].wait(vect)
                        vect.tensor_scalar(epsc[:, 1:2], epsc[:, 0:1],
                                           LNEPS / D, LNEPS * 1e-6,
                                           ALU.mult, ALU.add)
                        S["s_lnv"].inc(vect.drain())
                        S["s_xn2"].wait(vect)
                        S["s_x2w"].wait(vect)
                        emit_ln(apg[:, tci, :], D, x2t[:], eps_col=epsc[:, 1:2])
                        S["s_x2"].wait(sync)
                        S["s_lnv"].wait(sync)
                        S["s_xload"].dma(sync.dma_start(
                            out=xl2[:], in_=x_in[128 * tc:128 * (tc + 1), :]))
                        vect.tensor_scalar(apg[:, tci, :], apg[:, tci, :],
                                           stat[:, 6:7], stat[:, 7:8],
                                           ALU.mult, ALU.add)
                        S["s_aps"].inc(vect.drain())
                        S["s_xload"].wait(vect)
                        vect.tensor_tensor(x2t[:], apg[:, tci, :], xl2[:], ALU.add)
                        S["s_x2"].inc(vect.drain())
                        S["s_x2"].wait(sync)
                        S["s_x2w"].dma(sync.dma_start(
                            out=x2sp.ap()[128 * tc:128 * (tc + 1), :], in_=x2t[:]))
                        emit_ln(x2t[:], D, xl2[:])
                        S["s_x2w"].wait(vect)
                        vect.tensor_scalar(x2t[:], x2t[:], stat[:, 6:7],
                                           stat[:, 7:8], ALU.mult, ALU.add)
                        S["s_xn2"].inc(vect.drain())
                        S["s_xn2"].wait(tens)
                        for g in range(FC // 4):
                            bk = banks[4 + g % 2]
                            bk.acquire(tens)
                            for q in range(4):
                                fc = 4 * g + q
                                tens.transpose(
                                    bk.t[:, 128 * q:128 * (q + 1)],
                                    x2t[:, 128 * fc:128 * (fc + 1)], idn[:])
                            bk.produced(tens.drain())
                            bk.consumed_wait(vect)
                            pv = bk.t[:].rearrange("p (g c) -> p g c", c=128)
                            vect.tensor_copy(xnh[:, 4 * g:4 * g + 4,
                                                 128 * tc:128 * (tc + 1)], pv)
                            vect.drain()
                            vect.tensor_tensor(xnl[:, 4 * g:4 * g + 4,
                                                  128 * tc:128 * (tc + 1)], pv,
                                               xnh[:, 4 * g:4 * g + 4,
                                                   128 * tc:128 * (tc + 1)],
                                               ALU.subtract)
                            bk.release(vect.drain())
                        S["s_xn2"].inc(vect.drain())

        t_ = tap("x2", [TOK, D], F32)
        if t_ is not None:
            S["s_x2w"].wait(sync)
            S["s_out"].dma(sync.dma_start(out=t_, in_=x2sp.ap()[:, :]))
        t_ = tap("xn2h", [128, FC, TOK], BF16)
        if t_ is not None:
            S["s_xn2"].wait(sync)
            S["s_out"].dma(sync.dma_start(out=t_, in_=xnh[:]))
            S["s_out"].dma(sync.dma_start(out=tap("xn2l", [128, FC, TOK], BF16),
                                          in_=xnl[:]))
        t_ = tap("wgi", [2, D, D // 8], BF16)
        if t_ is not None:
            S["s_ws2"].wait(sync)
            S["s_out"].dma(sync.dma_start(out=t_, in_=agwgi.ap()[:]))
        t_ = tap("wgall", [NCORES, 2, D, D // 8], BF16)
        if t_ is not None:
            sync.wait_ge(S["s_ag"].h, k_agwg)
            S["s_out"].dma(sync.dma_start(out=t_, in_=agwgo.ap()[:]))
        t_ = tap("what0", [D, 2 * HFF // 8], BF16)
        if t_ is not None:
            sync.wait_ge(S["s_ag"].h, k_aggu)
            S["s_out"].dma(sync.dma_start(out=t_, in_=agguo.ap()[:, 0].rearrange("g r c -> (g r) c")))
        t_ = tap("whatp0", [HFF, D // 8], BF16)
        if t_ is not None:
            sync.wait_ge(S["s_ag"].h, k_agpj)
            S["s_out"].dma(sync.dma_start(out=t_, in_=agpjo.ap()[:, 0].rearrange("g r c -> (g r) c")))

        with ExitStack() as pf:
            wgub = pf.enter_context(nc.sbuf_tensor("wgub", [128, 2, FC, 128], BF16))
            pjb = pf.enter_context(nc.sbuf_tensor("pjb", [128, NHS, NCORES, 256], BF16))
            hh = pf.enter_context(nc.sbuf_tensor("hh", [128, NHS, TOK], BF16))
            hl = pf.enter_context(nc.sbuf_tensor("hl", [128, NHS, TOK], BF16))
            hf = pf.enter_context(nc.sbuf_tensor("hf", [128, 512], F32))
            sg2 = pf.enter_context(nc.sbuf_tensor("sg2", [128, 512], F32))
            sq3 = pf.enter_context(nc.sbuf_tensor("sq3", [128, 512], F32))
            sq3d = pf.enter_context(nc.sbuf_tensor("sq3d", [128, 2, 512], F32))
            ysb = pf.enter_context(nc.sbuf_tensor("ysb", [128, FC, TOK], F32))
            ytm = pf.enter_context(nc.sbuf_tensor("ytm", [128, D], F32))
            x2l = pf.enter_context(nc.sbuf_tensor("x2l", [128, D], F32))

            aguv = [agguo.ap()[g].rearrange("c (fc p) r -> c p fc r", p=128)
                    for g in range(4)]
            gub_use = [0, 0]
            for hcx in range(NHC):
                S["s_yad"].wait(sync)
                sync.wait_ge(S["s_ag"].h, k_agpj)
                for hs in range(NHS):
                    hrow = HC * hcx + 128 * hs
                    for c in range(NCORES):
                        S["s_pjl"].dma(sync.dma_start(
                            out=pjb[:, hs, c, :],
                            in_=agpjo.ap()[hrow // (HFF // 2), c,
                                           hrow % (HFF // 2):
                                           hrow % (HFF // 2) + 128, :]))
                pj_ld = S["s_pjl"].n

                def issue_gu(k):
                    hs_, part_ = divmod(k, 2)
                    row = HC * hcx + 128 * hs_ + HFF * part_
                    sl_ = k % 2
                    S["s_qkv"].wait_at(sync, gub_use[sl_])
                    csh, loc = row // (2 * HFF // 8), row % (2 * HFF // 8)
                    for g in range(4):
                        WL[sl_].dma(sync.dma_start(
                            out=wgub[:, sl_, 4 * g:4 * (g + 1), :],
                            in_=aguv[g][csh, :, :, loc:loc + 128]))

                issue_gu(0)
                issue_gu(1)
                for hs in range(NHS):
                    for part in range(2):
                        k = 2 * hs + part
                        sl = k % 2
                        WL[sl].wait(tens)
                        S["s_xn2"].wait(tens)
                        for th in range(2):
                            bg = banks[2 * part + th]
                            bg.acquire(tens)
                            last = None
                            for fc in range(FC):
                                tens.matmul(bg.t[:], wgub[:, sl, fc, :],
                                            xnh[:, fc, 512 * th:512 * (th + 1)],
                                            start=(fc == 0), stop=False)
                                last = tens.matmul(
                                    bg.t[:], wgub[:, sl, fc, :],
                                    xnl[:, fc, 512 * th:512 * (th + 1)],
                                    start=False, stop=(fc == FC - 1))
                            bg.produced(last)
                        S["s_hh"].inc(tens.drain())
                        S["s_qkv"].inc(tens.drain())
                        gub_use[sl] = S["s_qkv"].n
                        if k + 2 < 2 * NHS:
                            issue_gu(k + 2)
                    for th in range(2):
                        bgg, bgu = banks[th], banks[2 + th]
                        bgg.consumed_wait(scal)
                        S["s_hsp"].wait(scal)
                        scal.activation(sg2[:], bgg.t[:], ACT.Sigmoid)
                        S["s_silu"].inc(scal.drain())
                        bgg.consumed_wait(vect)
                        bgu.consumed_wait(vect)
                        S["s_silu"].wait(vect)
                        vect.tensor_tensor(hf[:], sg2[:], bgg.t[:], ALU.mult)
                        bgg.release(vect.drain())
                        S["s_hsp"].inc(vect.drain())
                        vect.tensor_tensor(hf[:], hf[:], bgu.t[:], ALU.mult)
                        bgu.release(vect.drain())
                        vect.tensor_copy(hh[:, hs, 512 * th:512 * (th + 1)], hf[:])
                        vect.drain()
                        vect.tensor_tensor(hl[:, hs, 512 * th:512 * (th + 1)],
                                           hf[:], hh[:, hs, 512 * th:512 * (th + 1)],
                                           ALU.subtract)
                        S["s_hh"].inc(vect.drain())
                        if hs < NHS - 1:
                            vect.tensor_tensor(sq3[:], hf[:], hf[:], ALU.mult)
                            S["s_qsr"].inc(vect.drain())
                            bsq = banks[6]
                            bsq.acquire(tens)
                            S["s_qsr"].wait(tens)
                            bsq.produced(tens.matmul(bsq.t[:1, :512], onec[:],
                                                     sq3[:],
                                                     start=True, stop=True))
                            bsq.consumed_wait(vect)
                            vect.tensor_tensor(msqb[:, 512 * th:512 * (th + 1)],
                                               msqb[:, 512 * th:512 * (th + 1)],
                                               bsq.t[:1, :512], ALU.add)
                            bsq.release(vect.drain())
                        else:
                            vect.tensor_tensor(sq3d[:, th, :], hf[:], hf[:],
                                               ALU.mult)
                            S["s_qsr"].inc(vect.drain())
                if hcx == 0:
                    t_ = tap("hh0", [128, NHS, TOK], BF16)
                    if t_ is not None:
                        S["s_hh"].wait(sync)
                        S["s_out"].dma(sync.dma_start(out=t_, in_=hh[:]))
                        S["s_out"].dma(sync.dma_start(
                            out=tap("hl0", [128, NHS, TOK], BF16), in_=hl[:]))
                S["s_pjl"].wait_at(tens, pj_ld)
                S["s_hh"].wait(tens)
                for dc in range(FC):
                    for th in range(2):
                        bp = banks[4 + (2 * dc + th) % 2]
                        bp.acquire(tens)
                        last = None
                        for hs in range(NHS):
                            lt = pjb[:, hs, dc // 2, 128 * (dc % 2):128 * (dc % 2 + 1)]
                            tens.matmul(bp.t[:], lt,
                                        hh[:, hs, 512 * th:512 * (th + 1)],
                                        start=(hs == 0), stop=False)
                            last = tens.matmul(
                                bp.t[:], lt, hl[:, hs, 512 * th:512 * (th + 1)],
                                start=False, stop=(hs == NHS - 1))
                        bp.produced(last)
                        bp.consumed_wait(vect)
                        if hcx == 0:
                            vect.tensor_copy(
                                ysb[:, dc, 512 * th:512 * (th + 1)], bp.t[:])
                        else:
                            vect.tensor_tensor(
                                ysb[:, dc, 512 * th:512 * (th + 1)],
                                ysb[:, dc, 512 * th:512 * (th + 1)], bp.t[:], ALU.add)
                        bp.release(vect.drain())
                for th in range(2):
                    bsq = banks[6]
                    bsq.acquire(tens)
                    S["s_qsr"].wait(tens)
                    bsq.produced(tens.matmul(bsq.t[:1, :512], onec[:],
                                             sq3d[:, th, :],
                                             start=True, stop=True))
                    bsq.consumed_wait(vect)
                    vect.tensor_tensor(msqb[:, 512 * th:512 * (th + 1)],
                                       msqb[:, 512 * th:512 * (th + 1)],
                                       bsq.t[:1, :512], ALU.add)
                    bsq.release(vect.drain())
                S["s_yad"].inc(vect.drain())
                if hcx == 0:
                    t_ = tap("y0", [128, FC, TOK], F32)
                    if t_ is not None:
                        S["s_yad"].wait(sync)
                        S["s_out"].dma(sync.dma_start(out=t_, in_=ysb[:]))

            t_ = tap("ysb", [128, FC, TOK], F32)
            if t_ is not None:
                S["s_yad"].wait(sync)
                S["s_out"].dma(sync.dma_start(out=t_, in_=ysb[:]))
            S["s_yad"].wait(tens)
            for tc in range(TC):
                S["s_fin"].wait(vect)
                S["s_out"].wait(vect)
                for g in range(FC // 4):
                    bk = banks[6 + g % 2]
                    bk.acquire(tens)
                    for q in range(4):
                        fc = 4 * g + q
                        tens.transpose(bk.t[:, 128 * q:128 * (q + 1)],
                                       ysb[:, fc, 128 * tc:128 * (tc + 1)],
                                       idn[:])
                    bk.produced(tens.drain())
                    bk.consumed_wait(vect)
                    vect.tensor_copy(ytm[:, 512 * g:512 * (g + 1)], bk.t[:])
                    bk.release(vect.drain())
                S["s_fin"].inc(vect.drain())
                S["s_yad"].wait(sync)
                S["s_lnv"].wait(sync)
                S["s_eps"].dma(sync.dma_start(
                    out=epsc[:, 0:1], in_=msqb[0:1, 128 * tc:128 * (tc + 1)],
                    allow_slow_non_contiguous=True))
                S["s_eps"].wait(vect)
                vect.tensor_scalar(epsc[:, 1:2], epsc[:, 0:1],
                                   LNEPS / HFF, LNEPS * 1.1920928955078125e-07,
                                   ALU.mult, ALU.add)
                S["s_lnv"].inc(vect.drain())
                S["s_xn2"].wait(vect)
                S["s_xload"].wait(vect)
                emit_ln(ytm[:], D, x2l[:], eps_col=epsc[:, 1:2])
                S["s_x2w"].wait(sync)
                S["s_lnv"].wait(sync)
                S["s_fin"].wait(sync)
                S["s_xload"].dma(sync.dma_start(
                    out=x2l[:], in_=x2sp.ap()[128 * tc:128 * (tc + 1), :]))
                vect.tensor_scalar(ytm[:], ytm[:], stat[:, 6:7], stat[:, 7:8],
                                   ALU.mult, ALU.add)
                S["s_xn2"].inc(vect.drain())
                S["s_xload"].wait(vect)
                vect.tensor_tensor(ytm[:], ytm[:], x2l[:], ALU.add)
                S["s_fin"].inc(vect.drain())
                S["s_fin"].wait(sync)
                S["s_out"].dma(sync.dma_start(
                    out=out_ext[128 * tc:128 * (tc + 1), :], in_=ytm[:]))

        S["s_out"].wait(gps)

    es.close()
    return nc, tap_aps


_NC_CACHE = {}


def _get_program(taps=()):
    key = tuple(sorted(taps))
    if key not in _NC_CACHE:
        _NC_CACHE[key] = build_program(taps)
    return _NC_CACHE[key]


def make_in_maps(x, slope, normw, wqkv, wg, wout, wgu, wpj):
    in_maps = []
    for c in range(NCORES):
        b, r = c // 4, c % 4
        gaps = np.zeros(NCORES, np.float32)
        incm = np.zeros(NCORES, np.float32)
        for cp in range(NCORES):
            if cp // 4 == b and cp % 4 < r:
                incm[cp] = 1.0
                gaps[cp] = float(r - 1 - (cp % 4))
        in_maps.append({
            "x": np.ascontiguousarray(x[b, TOK * r:TOK * (r + 1)]),
            "slope": np.ascontiguousarray(slope.reshape(1, NH)),
            "normw": np.ascontiguousarray(normw),
            "wq0": np.ascontiguousarray(wqkv[384 * c:384 * (c + 1)]),
            "wq1": np.ascontiguousarray(wqkv[3072 + 384 * c:3072 + 384 * (c + 1)]),
            "wg": np.ascontiguousarray(wg[256 * c:256 * (c + 1)]),
            "wo": np.ascontiguousarray(wout[256 * c:256 * (c + 1)]),
            "wgu": np.ascontiguousarray(wgu[2048 * c:2048 * (c + 1)]),
            "wpj": np.ascontiguousarray(wpj[256 * c:256 * (c + 1)]),
            "gaps": gaps.reshape(1, NCORES),
            "incm": incm.reshape(1, NCORES),
        })
    return in_maps


def kernel(L0_attn_hidden_states, L0_attn_slope_rate, L0_attn_w_qkv,
           L0_attn_w_gate, L0_attn_w_out, L0_attn_norm_w,
           L0_ffn_w_gate_up, L0_ffn_w_proj):
    x = np.ascontiguousarray(np.asarray(L0_attn_hidden_states, dtype=np.float32))
    nc, _ = _get_program()
    in_maps = make_in_maps(
        x, np.asarray(L0_attn_slope_rate, dtype=np.float32),
        np.asarray(L0_attn_norm_w, dtype=np.float32),
        np.asarray(L0_attn_w_qkv, dtype=np.float32),
        np.asarray(L0_attn_w_gate, dtype=np.float32),
        np.asarray(L0_attn_w_out, dtype=np.float32),
        np.asarray(L0_ffn_w_gate_up, dtype=np.float32),
        np.asarray(L0_ffn_w_proj, dtype=np.float32))
    res = run_bass_kernel_spmd(nc, in_maps, core_ids=list(range(NCORES)))
    out = np.stack([np.asarray(res.results[c]["out"]) for c in range(NCORES)])
    return out.reshape(2, 4096, 2048)
